# version 25
# baseline (speedup 1.0000x reference)
import numpy as np
import concourse.bass as bass
import concourse.mybir as mybir
from concourse.bass_utils import run_bass_kernel_spmd

F32 = mybir.dt.float32
BF16 = mybir.dt.bfloat16
AF = mybir.ActivationFunctionType
ALU = mybir.AluOpType
AX = mybir.AxisListType

D = 1024
DC = 8
FF = 4096
FC = 32
SUB = 512
NCORE = 8
EPS = 1e-5

ENGS = ("t", "v", "s", "g", "y")
ENG_SEM = {"t": "st", "v": "sv", "s": "ss", "g": "sg"}


class Prog:

    def __init__(self, nc, sems):
        self.nc = nc
        self.sems = sems
        self.cnt = {k: 0 for k in sems}
        self.items = {}
        self.phase = None
        self.last_wait = {e: {} for e in ENGS}
        self.res = {}

    def set_phase(self, ph):
        self.phase = ph
        if ph not in self.items:
            self.items[ph] = {e: [] for e in ENGS}

    def _wait(self, eng, ev):
        sem_name, val = ev
        if val <= 0:
            return
        assert val <= self.cnt[sem_name], (sem_name, val, self.cnt[sem_name])
        lw = self.last_wait[eng]
        if lw.get(sem_name, 0) >= val:
            return
        lw[sem_name] = val
        h = self.sems[sem_name]
        self.items[self.phase][eng].append(lambda e, h=h, v=val: e.wait_ge(h, v))

    def op(self, eng, fn, reads=(), writes=(), dma=False, dma_sem=None, extra_deps=()):
        if dma:
            assert dma_sem is not None
            sem_name = dma_sem
        else:
            sem_name = ENG_SEM[eng]
        amt = 16 if dma else 1
        deps = []
        for r in reads:
            st = self.res.get(r)
            if st and st[0] is not None:
                deps.append(st[0])
        for r in writes:
            st = self.res.get(r)
            if st:
                if st[0] is not None and st[0][0] != sem_name:
                    deps.append(st[0])
                for s_, v_ in st[1].items():
                    deps.append((s_, v_))
        for d in extra_deps:
            deps.append(d)
        byname = {}
        for s_, v_ in deps:
            byname[s_] = max(byname.get(s_, 0), v_)
        for s_, v_ in sorted(byname.items()):
            self._wait(eng, (s_, v_))
        self.cnt[sem_name] += amt
        ev = (sem_name, self.cnt[sem_name])
        h = self.sems[sem_name]
        def item(e, fn=fn, h=h, amt=amt):
            last = fn(e)
            last.then_inc(h, amt)
        self.items[self.phase][eng].append(item)
        for r in writes:
            self.res[r] = [ev, {}]
        for r in reads:
            if r in writes:
                continue
            st = self.res.setdefault(r, [None, {}])
            st[1][ev[0]] = max(st[1].get(ev[0], 0), ev[1])
        return ev

    def raw_wait(self, eng, sem_name, val):
        self._wait(eng, (sem_name, val))

    def replay(self, phase, block):
        reg = {"t": block.tensor, "v": block.vector, "s": block.scalar,
               "g": block.gpsimd, "y": block.sync}
        for e in ENGS:
            items = self.items[phase][e]
            reg[e](lambda eng, items=items: [it(eng) for it in items])


def build_core_program(nsub=16, debug=False):
    T = nsub * SUB
    nc = bass.Bass("TRN2", target_bir_lowering=False)

    x_t = nc.dram_tensor("x_t", [D, T], F32, kind="ExternalInput")
    wq_t = nc.dram_tensor("wq_t", [D, D], F32, kind="ExternalInput")
    wk_t = nc.dram_tensor("wk_t", [D, D], F32, kind="ExternalInput")
    wv_t = nc.dram_tensor("wv_t", [D, D], F32, kind="ExternalInput")
    wg_t = nc.dram_tensor("wg_t", [D, D], F32, kind="ExternalInput")
    wo_t = nc.dram_tensor("wo_t", [D, D], F32, kind="ExternalInput")
    w1_t = nc.dram_tensor("w1_t", [D, FF], F32, kind="ExternalInput")
    w2_t = nc.dram_tensor("w2_t", [FF, D], F32, kind="ExternalInput")
    b1_t = nc.dram_tensor("b1_t", [128, FC], F32, kind="ExternalInput")
    b2_t = nc.dram_tensor("b2_t", [128, DC], F32, kind="ExternalInput")
    ldec_t = nc.dram_tensor("ldec_t", [1, 1], F32, kind="ExternalInput")
    cdiff_t = nc.dram_tensor("cdiff_t", [128, 128], F32, kind="ExternalInput")
    cind_t = nc.dram_tensor("cind_t", [128, 128], F32, kind="ExternalInput")
    out_t = nc.dram_tensor("out_t", [D, T], F32, kind="ExternalOutput")
    ikind = "ExternalOutput" if debug else "Internal"
    dbg = {}
    if debug:
        for nm in ("dxn", "dq", "dk", "dg", "dgo", "dxf1", "dxf2"):
            dbg[nm] = nc.dram_tensor(nm, [D, SUB], F32, kind="ExternalOutput")
        dbg["dv"] = nc.dram_tensor("dv", [128, 4 * D], F32, kind="ExternalOutput")
        dbg["dsm"] = nc.dram_tensor("dsm", [128, 4 * 128], F32, kind="ExternalOutput")
        dbg["dmask"] = nc.dram_tensor("dmask", [128, 128], F32, kind="ExternalOutput")
    xmid = nc.dram_tensor("xmid", [D, T], BF16, kind=ikind)
    w1q = nc.dram_tensor("w1q", [4 * D * 1024], BF16, kind=ikind)
    w2q = nc.dram_tensor("w2q", [FF, D], BF16, kind=ikind)
    thr_scr = nc.dram_tensor("thr_scr", [2], F32, kind="Internal")

    import contextlib
    stk = contextlib.ExitStack()
    with stk:
        sems = {}
        for s in ("st", "sv", "ss", "sg",
                  "yc", "yx0", "yx1", "yb0", "yb1", "yg0", "yg1",
                  "yw2a", "yw2b", "yw2c", "yw2d", "yo0", "yo1", "ydbg",
                  "gg0", "gg1", "gth",
                  "gw0", "gw1", "gw2", "gw3", "gw4", "gb0", "gb1",
                  "gs0", "gs1", "gs2", "gt0", "gt1", "gm0", "gm1", "gdbg"):
            sems[s] = stk.enter_context(nc.semaphore(s))

        pso2 = stk.enter_context(nc.psum_tensor("pso2", [128, SUB], F32))
        stsq = stk.enter_context(nc.psum_tensor("stsq", [128, SUB], F32))
        psm = stk.enter_context(nc.psum_tensor("psm", [128, SUB], F32))
        psp = [stk.enter_context(nc.psum_tensor(f"psp{j}", [128, SUB], F32)) for j in (0, 1)]
        pss = stk.enter_context(nc.psum_tensor("pss", [128, 4, 128], F32))
        pso = [stk.enter_context(nc.psum_tensor(f"pso{j}", [128, SUB], F32)) for j in (0, 1)]

        ones1 = stk.enter_context(nc.sbuf_tensor("ones1", [1, 128], BF16))
        ones1f = stk.enter_context(nc.sbuf_tensor("ones1f", [1, 128], F32))
        oneM = stk.enter_context(nc.sbuf_tensor("oneM", [128, 128], BF16))
        onecol = stk.enter_context(nc.sbuf_tensor("onecol", [128, 1], F32))
        mask = stk.enter_context(nc.sbuf_tensor("mask", [128, 128], F32))
        b1s = stk.enter_context(nc.sbuf_tensor("b1s", [128, FC], F32))
        b2s = stk.enter_context(nc.sbuf_tensor("b2s", [128, DC], F32))
        rows_b = stk.enter_context(nc.sbuf_tensor("rows_b", [128, SUB], BF16))
        rows_c = stk.enter_context(nc.sbuf_tensor("rows_c", [128, SUB], BF16))
        ldec = stk.enter_context(nc.sbuf_tensor("ldec", [1, 1], F32))
        lncol = stk.enter_context(nc.sbuf_tensor("lncol", [128, 1], F32))
        thr = stk.enter_context(nc.sbuf_tensor("thr", [1, 2], F32))
        epsc = stk.enter_context(nc.sbuf_tensor("epsc", [128, 1], F32))
        thr_col = stk.enter_context(nc.sbuf_tensor("thr_col", [128, 2], F32))
        nthr_col = stk.enter_context(nc.sbuf_tensor("nthr_col", [128, 2], F32))

        cdiff = stk.enter_context(nc.sbuf_tensor("cdiff", [128, 128], F32))
        cind = stk.enter_context(nc.sbuf_tensor("cind", [128, 128], F32))

        P = Prog(nc, sems)

        def replay(phase):
            with nc.Block() as block:
                P.replay(phase, block)

        def bc3(ps, nch):
            return bass.AP(ps, 0, [[SUB, 128], [0, nch], [1, SUB]])

        P.set_phase("p1")
        P.op("y", lambda e: e.dma_start(b1s[:], b1_t.ap()), writes=["b1s"], dma=True, dma_sem="yc")
        P.op("y", lambda e: e.dma_start(b2s[:], b2_t.ap()), writes=["b2s"], dma=True, dma_sem="yc")
        P.op("y", lambda e: e.dma_start(ldec[:], ldec_t.ap()), writes=["ldec"], dma=True, dma_sem="yc")
        P.op("y", lambda e: e.dma_start(cdiff[:], cdiff_t.ap()), writes=["cdiff"], dma=True, dma_sem="yc")
        P.op("y", lambda e: e.dma_start(cind[:], cind_t.ap()), writes=["cind"], dma=True, dma_sem="yc")
        for _r in ("b1s", "b2s", "ldec", "cdiff", "cind"):
            P.res[_r][0] = ("yc", P.cnt["yc"])
        P.op("v", lambda e: e.memset(ones1[:], 1.0), writes=["ones1"])
        P.op("v", lambda e: e.memset(ones1f[:], 1.0), writes=["ones1f"])
        P.op("v", lambda e: e.memset(oneM[:], 1.0 / D), writes=["oneM"])
        P.op("v", lambda e: e.memset(onecol[:], 1.0), writes=["onecol"])
        P.op("v", lambda e: e.memset(epsc[:], EPS), writes=["epsc"])
        P.op("s", lambda e: e.activation(thr[0:1, 0:1], ldec[:], AF.Sigmoid, bias=0.0, scale=1.0),
             reads=["ldec"], writes=["thr"])
        P.op("s", lambda e: e.activation(thr[0:1, 1:2], thr[0:1, 0:1], AF.Ln, bias=0.0, scale=1.0),
             reads=["thr"], writes=["thr"])
        P.op("t", lambda e: e.matmul(psm[:, 0:1], ones1f[:], thr[0:1, 1:2], start=True, stop=True),
             reads=["thr", "ones1f"], writes=["psm"])
        P.op("v", lambda e: e.tensor_copy(lncol[:], psm[:, 0:1]), reads=["psm"], writes=["lncol"])
        P.op("v", lambda e: e.tensor_scalar(mask[:], cdiff[:], lncol[:], None, ALU.mult),
             reads=["cdiff", "lncol"], writes=["mask"])
        P.op("s", lambda e: e.activation(cdiff[:], mask[:], AF.Exp, bias=0.0, scale=1.0),
             reads=["mask"], writes=["cdiff"])
        P.op("v", lambda e: e.tensor_tensor(mask[:], cdiff[:], cind[:], ALU.mult),
             reads=["cdiff", "cind"], writes=["mask"])


        with contextlib.ExitStack() as p1:
            ws = {}
            for nm, dt_ in (("wq", wq_t), ("wk", wk_t), ("wv", wv_t), ("wg", wg_t), ("wo", wo_t)):
                ws[nm] = p1.enter_context(nc.sbuf_tensor(nm + "s", [128, DC, D], BF16))
            xf = [p1.enter_context(nc.sbuf_tensor(f"xf{j}", [128, DC, SUB], F32)) for j in (0, 1)]
            xbs = [p1.enter_context(nc.sbuf_tensor(f"xbs{j}", [128, DC, SUB], BF16)) for j in (0, 1)]
            sq = p1.enter_context(nc.sbuf_tensor("sq", [128, DC, SUB], BF16))
            xns = [p1.enter_context(nc.sbuf_tensor(f"xns{j}", [128, DC, SUB], BF16)) for j in (0, 1)]
            qs = p1.enter_context(nc.sbuf_tensor("qs", [128, DC, SUB], BF16))
            ks = p1.enter_context(nc.sbuf_tensor("ks", [128, DC, SUB], BF16))
            gs_ = p1.enter_context(nc.sbuf_tensor("gs_", [128, DC, SUB], BF16))
            vs = p1.enter_context(nc.sbuf_tensor("vs", [128, 4, D], BF16))
            sm = p1.enter_context(nc.sbuf_tensor("sm", [128, 4, 128], BF16))
            stg = [p1.enter_context(nc.sbuf_tensor(f"stg{j}", [128, 1024], F32)) for j in range(3)]
            wt = [p1.enter_context(nc.sbuf_tensor(f"wt{j}", [128, 1024], BF16)) for j in (0, 1)]
            pacc = p1.enter_context(nc.sbuf_tensor("pacc", [128, 2], F32))
            pcol = p1.enter_context(nc.sbuf_tensor("pcol", [128, 1], F32))

            P.set_phase("p1")

            def x_ap_early(i):
                return bass.AP(x_t, i * SUB, [[T, 128], [128 * T, DC], [1, SUB]])

            def load_xb(j):
                P.op("g", lambda e, j=j: e.dma_start(xbs[j % 2][:], x_ap_early(j)),
                     writes=[f"xbs{j % 2}"], dma=True, dma_sem=f"gb{j % 2}")

            load_xb(0)
            if nsub > 1:
                load_xb(1)
            for wi, (nm, dt_) in enumerate((("wq", wq_t), ("wk", wk_t), ("wg", wg_t),
                                            ("wv", wv_t), ("wo", wo_t))):
                ap = bass.AP(dt_, 0, [[D, 128], [128 * D, DC], [1, D]])
                P.op("g", lambda e, s=ws[nm], ap=ap: e.dma_start(s[:], ap),
                     writes=[nm], dma=True, dma_sem=f"gw{wi}")

            psteps = []
            tiles = []
            for dc in range(8):
                for qg in range(4):
                    ap_in = bass.AP(w1_t, dc * 128 * FF + qg * 1024, [[FF, 128], [1, 1024]])
                    ap_out = bass.AP(w1q, qg * D * 1024 + dc * 128 * 1024, [[1024, 128], [1, 1024]])
                    tiles.append((0, ap_in, ap_out))
            for fc in range(32):
                ap_in = bass.AP(w2_t, fc * 128 * D, [[D, 128], [1, 1024]])
                ap_out = bass.AP(w2q, fc * 128 * D, [[D, 128], [1, 1024]])
                tiles.append((1, ap_in, ap_out))
            ntile = len(tiles)

            def p1_load(tix):
                w, ap_in, _ = tiles[tix]
                tb = tix % 3
                P.op("g", lambda e, s=stg[tb], ap=ap_in: e.dma_start(s[:], ap),
                     writes=[f"stg{tb}"], dma=True, dma_sem=f"gs{tb}")

            def p1_red(tix):
                tb = tix % 3
                w = tiles[tix][0]
                P.op("v", lambda e, tb=tb: e.tensor_reduce(
                    pcol[:], stg[tb][:], AX.X, ALU.add, apply_absolute_value=True),
                    reads=[f"stg{tb}"], writes=["pcol"])
                P.op("v", lambda e, w=w: e.tensor_tensor(
                    pacc[:, w:w + 1], pacc[:, w:w + 1], pcol[:], ALU.add),
                    reads=["pcol", "pacc"], writes=["pacc"])

            for tix in range(2):
                psteps.append(lambda tix=tix: p1_load(tix))
            for tix in range(ntile):
                psteps.append(lambda tix=tix: p1_red(tix))
                if tix + 2 < ntile:
                    psteps.append(lambda tix=tix: p1_load(tix + 2))

            def theta_chain():
                P.op("g", lambda e: e.tensor_reduce(thr[:], pacc[:], AX.C, ALU.add),
                     reads=["pacc"], writes=["thr"])
                P.op("v", lambda e: e.tensor_scalar(thr[:], thr[:], 0.5 / (FF * D), 0.5e-8,
                                                    ALU.mult, ALU.max),
                     reads=["thr"], writes=["thr"])
                P.op("g", lambda e: e.dma_start(thr_scr.ap(), thr[:]),
                     reads=["thr"], writes=["thr_scr"], dma=True, dma_sem="gth")
                P.op("g", lambda e: e.dma_start(thr_col[:], bass.AP(thr_scr, 0, [[0, 128], [1, 2]])),
                     reads=["thr_scr"], writes=["thr_col"], dma=True, dma_sem="gth")
                P.op("v", lambda e: e.tensor_scalar(nthr_col[:], thr_col[:], -1.0, None, ALU.mult),
                     reads=["thr_col"], writes=["nthr_col"])
            psteps.append(theta_chain)

            def p2_load(tix):
                tb = tix % 3
                _, ap_in, _ = tiles[tix]
                P.op("g", lambda e, s=stg[tb], ap=ap_in: e.dma_start(s[:], ap),
                     writes=[f"stg{tb}"], dma=True, dma_sem=f"gs{tb}")

            def p2_tern(tix):
                w, _, ap_out = tiles[tix]
                tb = tix % 3
                wb2 = tix % 2
                P.op("v", lambda e, tb=tb, wb2=wb2, w=w: e.tensor_scalar(
                    wt[wb2][:], stg[tb][:], nthr_col[:, w:w + 1], None, ALU.is_lt),
                    reads=[f"stg{tb}", "nthr_col"], writes=[f"wt{wb2}"])
                P.op("v", lambda e, tb=tb, wb2=wb2, w=w: e.scalar_tensor_tensor(
                    wt[wb2][:], stg[tb][:], thr_col[:, w:w + 1], wt[wb2][:], ALU.is_gt, ALU.subtract),
                    reads=[f"stg{tb}", "thr_col", f"wt{wb2}"], writes=[f"wt{wb2}"])
                P.op("g", lambda e, wb2=wb2, ap=ap_out: e.dma_start(ap, wt[wb2][:]),
                     reads=[f"wt{wb2}"], writes=["w1q" if w == 0 else "w2q"], dma=True, dma_sem=f"gt{wb2}")

            for tix in range(2):
                psteps.append(lambda tix=tix: p2_load(tix))
            for tix in range(ntile):
                psteps.append(lambda tix=tix: p2_tern(tix))
                if tix + 2 < ntile:
                    psteps.append(lambda tix=tix: p2_load(tix + 2))

            pq = list(psteps)
            per_sub = -(-len(pq) // max(1, nsub - 1))

            def x_ap(i):
                return bass.AP(x_t, i * SUB, [[T, 128], [128 * T, DC], [1, SUB]])

            def xmid_ap(i):
                return bass.AP(xmid, i * SUB, [[T, 128], [128 * T, DC], [1, SUB]])

            def load_x(i):
                P.op("y", lambda e, i=i: e.dma_start(xf[i % 2][:], x_ap(i)),
                     writes=[f"xf{i % 2}"], dma=True, dma_sem=f"yx{i % 2}")

            load_x(0)
            if nsub > 1:
                load_x(1)

            def mmst(e, src_, dst):
                last = None
                for kc in range(DC):
                    last = e.matmul(dst[:], oneM[:], src_[:, kc, :],
                                    start=(kc == 0), stop=(kc == DC - 1))
                return last

            def bc3sb(t_):
                return bass.AP(t_, 0, [[SUB, 128], [0, DC], [1, SUB]])

            def ln_a(j):
                xbj, xbr = xbs[j % 2], f"xbs{j % 2}"
                P.op("v", lambda e, xbj=xbj: e.tensor_tensor(sq[:], xbj[:], xbj[:], ALU.mult),
                     reads=[xbr], writes=["sq"])
                P.op("t", lambda e, xbj=xbj: mmst(e, xbj, psm), reads=[xbr, "oneM"], writes=["psm"])
                P.op("t", lambda e: mmst(e, sq, stsq), reads=["sq", "oneM"], writes=["stsq"])
                P.op("s", lambda e: e.activation(rows_b[:], psm[:], AF.Square, bias=0.0, scale=1.0),
                     reads=["psm"], writes=["rows_b"])
                P.op("v", lambda e: e.scalar_tensor_tensor(rows_c[:], stsq[:], 0.0, rows_b[:],
                                                           ALU.add, ALU.subtract),
                     reads=["stsq", "rows_b"], writes=["rows_c"])
                P.op("s", lambda e: e.activation(rows_b[:], rows_c[:], AF.Ln, bias=epsc[:], scale=1.0),
                     reads=["rows_c", "epsc"], writes=["rows_b"])
                P.op("s", lambda e: e.activation(rows_c[:], rows_b[:], AF.Exp, bias=0.0, scale=-0.5),
                     reads=["rows_b"], writes=["rows_c"])

            def ln_b(j):
                pass

            def ln_c(j):
                xbj, xbr = xbs[j % 2], f"xbs{j % 2}"
                P.op("v", lambda e, xbj=xbj: e.tensor_tensor(sq[:], xbj[:], bc3(psm, DC), ALU.subtract),
                     reads=[xbr, "psm"], writes=["sq"])

            def ln_d(j):
                xnj, xnr = xns[j % 2], f"xns{j % 2}"
                P.op("v", lambda e, xnj=xnj: e.tensor_tensor(xnj[:], sq[:], bc3sb(rows_c), ALU.mult),
                     reads=["sq", "rows_c"], writes=[xnr])

            def ln_cluster(j):
                ln_a(j); ln_b(j); ln_c(j); ln_d(j)

            def dump_fm(nm, sb, res):
                ap = bass.AP(dbg[nm], 0, [[SUB, 128], [128 * SUB, DC], [1, SUB]])
                P.op("g", lambda e, sb=sb, ap=ap: e.dma_start(ap, sb[:]), reads=[res], dma=True, dma_sem="gdbg")

            def dump_raw(nm, sb_ap, res):
                P.op("g", lambda e, sb_ap=sb_ap, nm=nm: e.dma_start(dbg[nm].ap(), sb_ap), reads=[res], dma=True, dma_sem="gdbg")

            if debug:
                dump_raw("dmask", mask[:], "mask")

            P.op("v", lambda e: e.memset(pacc[:], 0.0), writes=["pacc"])
            ln_cluster(0)
            if nsub > 1:
                ln_cluster(1)
            if nsub > 2:
                load_xb(2)

            for i in range(nsub):
                buf = i % 2
                xfb = xf[buf]
                xfr = f"xf{buf}"
                xn = xns[buf]
                xnr = f"xns{buf}"
                if i + 3 < nsub:
                    load_xb(i + 3)
                if debug and i == 0:
                    dump_fm("dxn", xn, xnr)
                P.op("v", lambda e, xfb=xfb, xn=xn: e.tensor_tensor(xfb[:], xfb[:], xn[:], ALU.add),
                     reads=[xnr, xfr], writes=[xfr])
                if debug and i == 0:
                    dump_fm("dxf1", xfb, xfr)
                    ap2 = bass.AP(dbg["dxf2"], 0, [[SUB, 128], [128 * SUB, DC], [1, SUB]])
                    P.op("y", lambda e, xfb=xfb, ap2=ap2: e.dma_start(ap2, xfb[:]), reads=[xfr], dma=True, dma_sem="ydbg")
                pbanks = [(psp[0], "psp0"), (psp[1], "psp1"), (pso[0], "pso0"),
                          (pso[1], "pso1"), (pso2, "pso2")]
                def proj(wsb, wres, dst, dres, sigmoid=False, per_chunk=False):
                    for do in range(DC):
                        pb, pbr = pbanks[do % 5]
                        def mmg(e, wsb=wsb, do=do, pb=pb, xn=xn):
                            last = None
                            for kc in range(DC):
                                last = e.matmul(pb[:], wsb[:, kc, do * 128:(do + 1) * 128],
                                                xn[:, kc, :], start=(kc == 0), stop=(kc == DC - 1))
                            return last
                        P.op("t", mmg, reads=[xnr, wres], writes=[pbr])
                        wr = [f"{dres}{do}"] if per_chunk else [dres]
                        if sigmoid:
                            P.op("s", lambda e, do=do, pb=pb, dst=dst: e.activation(
                                dst[:, do, :], pb[:], AF.Sigmoid, bias=0.0, scale=1.0),
                                reads=[pbr], writes=wr)
                        else:
                            P.op("s", lambda e, do=do, pb=pb, dst=dst: e.copy(dst[:, do, :], pb[:]),
                                 reads=[pbr], writes=wr)
                proj(ws["wq"], "wq", qs, "qs", per_chunk=True)
                proj(ws["wk"], "wk", ks, "ks")
                if debug and i == 0:
                    dump_fm("dq", qs, "qs")
                    dump_fm("dk", ks, "ks")
                    dump_fm("dg", gs_, "gs")
                for grp in range(4):
                    for half in range(2):
                        pb, pbr = pbanks[(grp * 2 + half) % 5]
                        def mmv(e, grp=grp, half=half, pb=pb, xn=xn):
                            last = None
                            for kc in range(DC):
                                last = e.matmul(pb[:], xn[:, kc, grp * 128:(grp + 1) * 128],
                                                ws["wv"][:, kc, half * 512:(half + 1) * 512],
                                                start=(kc == 0), stop=(kc == DC - 1))
                            return last
                        P.op("t", mmv, reads=[xnr, "wv"], writes=[pbr])
                        P.op("s", lambda e, grp=grp, half=half, pb=pb: e.copy(
                            vs[:, grp, half * 512:(half + 1) * 512], pb[:]),
                            reads=[pbr], writes=["vs"])
                proj(ws["wg"], "wg", gs_, "gs", sigmoid=True)
                if i + 2 < nsub:
                    ln_a(i + 2)
                def mms(e):
                    last = None
                    for grp in range(4):
                        for kc in range(DC):
                            last = e.matmul(pss[:, grp, :], ks[:, kc, grp * 128:(grp + 1) * 128],
                                            qs[:, kc, grp * 128:(grp + 1) * 128],
                                            start=(kc == 0), stop=(kc == DC - 1))
                    return last
                P.op("t", mms, reads=[f"qs{c}" for c in range(DC)] + ["ks"], writes=["pss"])
                P.op("v", lambda e: e.tensor_tensor(
                    sm[:], pss[:], bass.AP(mask, 0, [[128, 128], [0, 4], [1, 128]]), ALU.mult),
                    reads=["pss", "mask"], writes=["sm"])
                if debug and i == 0:
                    dump_raw("dv", vs[:], "vs")
                    dump_raw("dsm", sm[:], "sm")
                obanks = [(pso[0][:], "pso0"), (pso[1][:], "pso1"),
                          (bass.AP(pss, 0, [[SUB, 128], [1, SUB]]), "pss"),
                          (pso2[:], "pso2")]
                for dc in range(DC):
                    pb, pbr = obanks[dc % 4]
                    def mmo(e, dc=dc, pb=pb):
                        last = None
                        for grp in range(4):
                            last = e.matmul(pb[:, grp * 128:(grp + 1) * 128],
                                            vs[:, grp, dc * 128:(dc + 1) * 128],
                                            sm[:, grp, :], start=True, stop=True)
                        return last
                    P.op("t", mmo, reads=["vs", "sm"], writes=[pbr])
                    P.op("v", lambda e, dc=dc, pb=pb: e.tensor_tensor(
                        qs[:, dc, :], pb[:], gs_[:, dc, :], ALU.mult),
                        reads=[pbr, "gs"], writes=[f"qs{dc}"])
                for do in range(DC):
                    pb = psp[do % 2]
                    pbr = f"psp{do % 2}"
                    for kc in range(DC):
                        P.op("t", lambda e, do=do, kc=kc, pb=pb: e.matmul(
                            pb[:], ws["wo"][:, kc, do * 128:(do + 1) * 128],
                            qs[:, kc, :], start=(kc == 0), stop=(kc == DC - 1)),
                            reads=[f"qs{kc}", "wo"], writes=[pbr])
                    P.op("v", lambda e, do=do, pb=pb, xfb=xfb: e.tensor_tensor(
                        xfb[:, do, :], pb[:], xfb[:, do, :], ALU.add),
                        reads=[pbr, xfr], writes=[xfr])
                if i + 2 < nsub:
                    ln_c(i + 2)
                    ln_d(i + 2)
                if debug and i == 0:
                    dump_fm("dgo", qs, "qs0")
                P.op("g", lambda e, i=i, xfb=xfb: e.dma_start(xmid_ap(i), xfb[:]),
                     reads=[xfr], writes=[f"xm{i}"], dma=True, dma_sem=f"gm{i % 2}")
                if i + 2 < nsub:
                    load_x(i + 2)
                for _ in range(per_sub):
                    if pq:
                        pq.pop(0)()
            while pq:
                pq.pop(0)()

            fence = []
            for s in ("gw0", "gw1", "gw2", "gw3", "gw4", "gb0", "gb1", "gth",
                      "gs0", "gs1", "gs2", "gt0", "gt1", "gm0", "gm1", "gdbg"):
                P.raw_wait("g", s, P.cnt[s])
                fence.append((s, P.cnt[s]))
            for s in ("yc", "yx0", "yx1", "ydbg"):
                P.raw_wait("y", s, P.cnt[s])
                fence.append((s, P.cnt[s]))
            replay("p1")

        with contextlib.ExitStack() as p2:
            w2s = p2.enter_context(nc.sbuf_tensor("w2s", [128, FC, D], BF16))
            w1g = [p2.enter_context(nc.sbuf_tensor(f"w1g{j}", [128, DC, 1024], BF16)) for j in (0, 1)]
            xb2 = [p2.enter_context(nc.sbuf_tensor(f"xb2{j}", [128, DC, SUB], BF16)) for j in (0, 1)]
            sq2 = p2.enter_context(nc.sbuf_tensor("sq2", [128, DC, SUB], BF16))
            xn2 = p2.enter_context(nc.sbuf_tensor("xn2", [128, DC, SUB], BF16))
            hs = p2.enter_context(nc.sbuf_tensor("hs", [128, FC, SUB], BF16))
            outst = p2.enter_context(nc.sbuf_tensor("outst", [128, DC, SUB], F32))

            P.set_phase("p2")

            def load_w1g(gix):
                tb = gix % 2
                ap = bass.AP(w1q, (gix % 4) * D * 1024, [[1024, 128], [128 * 1024, DC], [1, 1024]])
                P.op("g", lambda e, tb=tb, ap=ap: e.dma_start(w1g[tb][:], ap),
                     reads=["w1q"], writes=[f"w1g{tb}"], dma=True, dma_sem=f"gg{tb}", extra_deps=fence)

            def load_xb2(i):
                P.op("y", lambda e, i=i: e.dma_start(xb2[i % 2][:], xmid_ap(i)),
                     reads=[f"xm{i}"], writes=[f"xb2{i % 2}"], dma=True, dma_sem=f"yb{i % 2}", extra_deps=fence)

            def out_ap(i):
                return bass.AP(out_t, i * SUB, [[T, 128], [128 * T, DC], [1, SUB]])

            load_xb2(0)
            load_w1g(0)
            if nsub > 1:
                load_xb2(1)
            load_w1g(1)
            for qrt in range(4):
                ap = bass.AP(w2q, qrt * 8 * 128 * D, [[D, 128], [128 * D, 8], [1, D]])
                P.op("y", lambda e, qrt=qrt, ap=ap: e.dma_start(w2s[:, qrt * 8:(qrt + 1) * 8, :], ap),
                     reads=["w2q"], writes=[f"w2s{qrt}"], dma=True,
                     dma_sem="yw2" + "abcd"[qrt], extra_deps=fence)

            def ln2_square(xbuf, xres):
                P.op("v", lambda e, xbuf=xbuf: e.tensor_tensor(sq2[:], xbuf[:], xbuf[:], ALU.mult),
                     reads=[xres], writes=["sq2"])

            def ln2_stats(xbuf, xres):
                P.op("t", lambda e, xbuf=xbuf: mmst(e, xbuf, psm), reads=[xres, "oneM"], writes=["psm"])
                P.op("t", lambda e: mmst(e, sq2, stsq), reads=["sq2", "oneM"], writes=["stsq"])

            def ln2_stage1(xbuf, xres):
                ln2_square(xbuf, xres)
                ln2_stats(xbuf, xres)

            def ln2_rows_v():
                P.op("s", lambda e: e.activation(rows_b[:], psm[:], AF.Square, bias=0.0, scale=1.0),
                     reads=["psm"], writes=["rows_b"])
                P.op("v", lambda e: e.scalar_tensor_tensor(rows_c[:], stsq[:], 0.0, rows_b[:],
                                                           ALU.add, ALU.subtract),
                     reads=["stsq", "rows_b"], writes=["rows_c"])

            def ln2_rstd_s():
                P.op("s", lambda e: e.activation(rows_b[:], rows_c[:], AF.Ln, bias=epsc[:], scale=1.0),
                     reads=["rows_c", "epsc"], writes=["rows_b"])
                P.op("s", lambda e: e.activation(rows_c[:], rows_b[:], AF.Exp, bias=0.0, scale=-0.5),
                     reads=["rows_b"], writes=["rows_c"])

            def ln2_rows():
                ln2_rows_v()
                ln2_rstd_s()

            def ln2_m_sub(xbuf, xres):
                P.op("v", lambda e, xbuf=xbuf: e.tensor_tensor(sq2[:], xbuf[:], bc3(psm, DC), ALU.subtract),
                     reads=[xres, "psm"], writes=["sq2"])

            def ln2_r_mul():
                P.op("v", lambda e: e.tensor_tensor(xn2[:], sq2[:], bc3sb(rows_c), ALU.mult),
                     reads=["sq2", "rows_c"], writes=["xn2"])

            ln2_stage1(xb2[0], "xb20")
            ln2_rows()
            ln2_m_sub(xb2[0], "xb20")
            ln2_r_mul()

            for i in range(nsub):
                buf = i % 2
                xb2b = xb2[buf]
                xb2r = f"xb2{buf}"
                nxt = i + 1 < nsub
                nbuf = (i + 1) % 2
                if nxt:
                    ln2_square(xb2[nbuf], f"xb2{nbuf}")
                w1banks = [(psp[0][:], "psp0"), (psp[1][:], "psp1"),
                           (bass.AP(pss, 0, [[SUB, 128], [1, SUB]]), "pss"), (pso2[:], "pso2")]
                for fg in range(4):
                    gix = i * 4 + fg
                    wb = w1g[gix % 2]
                    wbr = f"w1g{gix % 2}"
                    for fcol in range(8):
                        fc = fg * 8 + fcol
                        pb, pbr = w1banks[fcol % 4]
                        def mmw1(e, wb=wb, fcol=fcol, pb=pb):
                            last = None
                            for kc in range(DC):
                                last = e.matmul(pb[:], wb[:, kc, fcol * 128:(fcol + 1) * 128],
                                                xn2[:, kc, :], start=(kc == 0), stop=(kc == DC - 1))
                            return last
                        P.op("t", mmw1, reads=["xn2", wbr], writes=[pbr])
                        P.op("s", lambda e, fc=fc, pb=pb: e.activation(
                            hs[:, fc, :], pb[:], AF.Gelu, bias=b1s[:, fc:fc + 1], scale=1.0),
                            reads=[pbr, "b1s"], writes=[f"hs{fc // 8}"])
                    if gix + 2 < nsub * 4:
                        load_w1g(gix + 2)
                    if nxt:
                        if fg == 0:
                            ln2_stats(xb2[nbuf], f"xb2{nbuf}")
                            ln2_rows_v()
                        elif fg == 2:
                            ln2_m_sub(xb2[nbuf], f"xb2{nbuf}")
                        elif fg == 3:
                            ln2_rstd_s()
                            ln2_r_mul()
                for do in range(DC):
                    pb = pso[do % 2]
                    pbr = f"pso{do % 2}"
                    for qrt in range(4):
                        def mmw2(e, do=do, pb=pb, qrt=qrt):
                            last = None
                            for fc in range(qrt * 8, (qrt + 1) * 8):
                                last = e.matmul(pb[:], w2s[:, fc, do * 128:(do + 1) * 128],
                                                hs[:, fc, :], start=(fc == 0), stop=(fc == FC - 1))
                            return last
                        P.op("t", mmw2, reads=[f"hs{qrt}", f"w2s{qrt}"], writes=[pbr])
                    P.op("v", lambda e, do=do, pb=pb, xb2b=xb2b: e.scalar_tensor_tensor(
                        outst[:, do, :], pb[:], b2s[:, do:do + 1], xb2b[:, do, :], ALU.add, ALU.add),
                        reads=[pbr, "b2s", xb2r], writes=[f"outst{do // 4}"])
                    if do == 3:
                        apo = bass.AP(out_t, i * SUB, [[T, 128], [128 * T, 4], [1, SUB]])
                        P.op("y", lambda e, apo=apo: e.dma_start(apo, outst[:, 0:4, :]),
                             reads=["outst0"], writes=[f"outA{i}"], dma=True, dma_sem="yo0")
                if i + 2 < nsub:
                    load_xb2(i + 2)
                apo = bass.AP(out_t, i * SUB + 4 * 128 * T, [[T, 128], [128 * T, 4], [1, SUB]])
                P.op("y", lambda e, apo=apo: e.dma_start(apo, outst[:, 4:8, :]),
                     reads=["outst1"], writes=[f"outB{i}"], dma=True, dma_sem="yo1")

            for s in ("yo0", "yo1", "yb0", "yb1", "yw2a", "yw2b", "yw2c", "yw2d"):
                P.raw_wait("y", s, P.cnt[s])
            replay("p2")
    return nc



def _host_consts():
    idx = np.arange(128)
    same = (idx[:, None] // 32) == (idx[None, :] // 32)
    diff = (idx[None, :] - idx[:, None]).astype(np.float32) * same
    ind = (same & (idx[None, :] >= idx[:, None])).astype(np.float32)
    return np.ascontiguousarray(diff.astype(np.float32)), np.ascontiguousarray(ind)


def kernel(L0_attn_x, L0_attn_W_q, L0_attn_W_k, L0_attn_W_v, L0_attn_W_g,
           L0_attn_W_o, L0_attn_log_decay, L0_ffn_W1, L0_ffn_b1, L0_ffn_W2,
           L0_ffn_b2):
    x = np.asarray(L0_attn_x, dtype=np.float32)
    B, Tb, Dm = x.shape
    per = B // NCORE
    cdiff, cind = _host_consts()
    common = {
        "wq_t": np.ascontiguousarray(np.asarray(L0_attn_W_q, np.float32).T),
        "wk_t": np.ascontiguousarray(np.asarray(L0_attn_W_k, np.float32).T),
        "wv_t": np.ascontiguousarray(np.asarray(L0_attn_W_v, np.float32).T),
        "wg_t": np.ascontiguousarray(np.asarray(L0_attn_W_g, np.float32).T),
        "wo_t": np.ascontiguousarray(np.asarray(L0_attn_W_o, np.float32).T),
        "w1_t": np.ascontiguousarray(np.asarray(L0_ffn_W1, np.float32).T),
        "w2_t": np.ascontiguousarray(np.asarray(L0_ffn_W2, np.float32).T),
        "b1_t": np.ascontiguousarray(np.asarray(L0_ffn_b1, np.float32).reshape(FC, 128).T),
        "b2_t": np.ascontiguousarray(np.asarray(L0_ffn_b2, np.float32).reshape(DC, 128).T),
        "ldec_t": np.ascontiguousarray(np.asarray(L0_attn_log_decay, np.float32)[0:1].reshape(1, 1)),
        "cdiff_t": cdiff,
        "cind_t": cind,
    }
    in_maps = []
    for c in range(NCORE):
        xs = np.ascontiguousarray(x[c * per:(c + 1) * per].reshape(per * Tb, Dm).T)
        m = dict(common)
        m["x_t"] = xs
        in_maps.append(m)
    nsub = (per * Tb) // SUB
    nc = build_core_program(nsub)
    res = run_bass_kernel_spmd(nc, in_maps, core_ids=list(range(NCORE)))
    outs = [np.asarray(res.results[c]["out_t"], dtype=np.float32).T.reshape(per, Tb, Dm)
            for c in range(NCORE)]
    return np.concatenate(outs, axis=0)


# version 26 (final)
# speedup vs baseline: 1.0068x; 1.0068x over previous
import numpy as np
import concourse.bass as bass
import concourse.mybir as mybir
from concourse.bass_utils import run_bass_kernel_spmd

F32 = mybir.dt.float32
BF16 = mybir.dt.bfloat16
AF = mybir.ActivationFunctionType
ALU = mybir.AluOpType
AX = mybir.AxisListType

D = 1024
DC = 8
FF = 4096
FC = 32
SUB = 512
NCORE = 8
EPS = 1e-5

ENGS = ("t", "v", "s", "g", "y")
ENG_SEM = {"t": "st", "v": "sv", "s": "ss", "g": "sg"}


class Prog:

    def __init__(self, nc, sems):
        self.nc = nc
        self.sems = sems
        self.cnt = {k: 0 for k in sems}
        self.items = {}
        self.phase = None
        self.last_wait = {e: {} for e in ENGS}
        self.res = {}

    def set_phase(self, ph):
        self.phase = ph
        if ph not in self.items:
            self.items[ph] = {e: [] for e in ENGS}

    def _wait(self, eng, ev):
        sem_name, val = ev
        if val <= 0:
            return
        assert val <= self.cnt[sem_name], (sem_name, val, self.cnt[sem_name])
        lw = self.last_wait[eng]
        if lw.get(sem_name, 0) >= val:
            return
        lw[sem_name] = val
        h = self.sems[sem_name]
        self.items[self.phase][eng].append(lambda e, h=h, v=val: e.wait_ge(h, v))

    def op(self, eng, fn, reads=(), writes=(), dma=False, dma_sem=None, extra_deps=()):
        if dma:
            assert dma_sem is not None
            sem_name = dma_sem
        else:
            sem_name = ENG_SEM[eng]
        amt = 16 if dma else 1
        deps = []
        for r in reads:
            st = self.res.get(r)
            if st and st[0] is not None:
                deps.append(st[0])
        for r in writes:
            st = self.res.get(r)
            if st:
                if st[0] is not None and st[0][0] != sem_name:
                    deps.append(st[0])
                for s_, v_ in st[1].items():
                    deps.append((s_, v_))
        for d in extra_deps:
            deps.append(d)
        byname = {}
        for s_, v_ in deps:
            byname[s_] = max(byname.get(s_, 0), v_)
        for s_, v_ in sorted(byname.items()):
            self._wait(eng, (s_, v_))
        self.cnt[sem_name] += amt
        ev = (sem_name, self.cnt[sem_name])
        h = self.sems[sem_name]
        def item(e, fn=fn, h=h, amt=amt):
            last = fn(e)
            last.then_inc(h, amt)
        self.items[self.phase][eng].append(item)
        for r in writes:
            self.res[r] = [ev, {}]
        for r in reads:
            if r in writes:
                continue
            st = self.res.setdefault(r, [None, {}])
            st[1][ev[0]] = max(st[1].get(ev[0], 0), ev[1])
        return ev

    def raw_wait(self, eng, sem_name, val):
        self._wait(eng, (sem_name, val))

    def replay(self, phase, block):
        reg = {"t": block.tensor, "v": block.vector, "s": block.scalar,
               "g": block.gpsimd, "y": block.sync}
        for e in ENGS:
            items = self.items[phase][e]
            reg[e](lambda eng, items=items: [it(eng) for it in items])


def build_core_program(nsub=16, debug=False):
    T = nsub * SUB
    nc = bass.Bass("TRN2", target_bir_lowering=False)

    x_t = nc.dram_tensor("x_t", [D, T], F32, kind="ExternalInput")
    wq_t = nc.dram_tensor("wq_t", [D, D], F32, kind="ExternalInput")
    wk_t = nc.dram_tensor("wk_t", [D, D], F32, kind="ExternalInput")
    wv_t = nc.dram_tensor("wv_t", [D, D], F32, kind="ExternalInput")
    wg_t = nc.dram_tensor("wg_t", [D, D], F32, kind="ExternalInput")
    wo_t = nc.dram_tensor("wo_t", [D, D], F32, kind="ExternalInput")
    w1_t = nc.dram_tensor("w1_t", [D, FF], F32, kind="ExternalInput")
    w2_t = nc.dram_tensor("w2_t", [FF, D], F32, kind="ExternalInput")
    b1_t = nc.dram_tensor("b1_t", [128, FC], F32, kind="ExternalInput")
    b2_t = nc.dram_tensor("b2_t", [128, DC], F32, kind="ExternalInput")
    ldec_t = nc.dram_tensor("ldec_t", [1, 1], F32, kind="ExternalInput")
    cdiff_t = nc.dram_tensor("cdiff_t", [128, 128], F32, kind="ExternalInput")
    cind_t = nc.dram_tensor("cind_t", [128, 128], F32, kind="ExternalInput")
    out_t = nc.dram_tensor("out_t", [D, T], F32, kind="ExternalOutput")
    ikind = "ExternalOutput" if debug else "Internal"
    dbg = {}
    if debug:
        for nm in ("dxn", "dq", "dk", "dg", "dgo", "dxf1", "dxf2"):
            dbg[nm] = nc.dram_tensor(nm, [D, SUB], F32, kind="ExternalOutput")
        dbg["dv"] = nc.dram_tensor("dv", [128, 4 * D], F32, kind="ExternalOutput")
        dbg["dsm"] = nc.dram_tensor("dsm", [128, 4 * 128], F32, kind="ExternalOutput")
        dbg["dmask"] = nc.dram_tensor("dmask", [128, 128], F32, kind="ExternalOutput")
    xmid = nc.dram_tensor("xmid", [D, T], BF16, kind=ikind)
    w1q = nc.dram_tensor("w1q", [4 * D * 1024], BF16, kind=ikind)
    w2q = nc.dram_tensor("w2q", [FF, D], BF16, kind=ikind)

    import contextlib
    stk = contextlib.ExitStack()
    with stk:
        sems = {}
        for s in ("st", "sv", "ss", "sg",
                  "yc", "yx0", "yx1", "yb0", "yb1", "yg0", "yg1",
                  "yw2a", "yw2b", "yw2c", "yw2d", "yo0", "yo1", "ydbg",
                  "gg0", "gg1",
                  "gw0", "gw1", "gw2", "gw3", "gw4", "gb0", "gb1",
                  "gs0", "gs1", "gs2", "gt0", "gt1", "gm0", "gm1", "gdbg"):
            sems[s] = stk.enter_context(nc.semaphore(s))

        pso2 = stk.enter_context(nc.psum_tensor("pso2", [128, SUB], F32))
        stsq = stk.enter_context(nc.psum_tensor("stsq", [128, SUB], F32))
        psm = stk.enter_context(nc.psum_tensor("psm", [128, SUB], F32))
        psp = [stk.enter_context(nc.psum_tensor(f"psp{j}", [128, SUB], F32)) for j in (0, 1)]
        pss = stk.enter_context(nc.psum_tensor("pss", [128, 4, 128], F32))
        pso = [stk.enter_context(nc.psum_tensor(f"pso{j}", [128, SUB], F32)) for j in (0, 1)]

        ones1 = stk.enter_context(nc.sbuf_tensor("ones1", [1, 128], BF16))
        ones1f = stk.enter_context(nc.sbuf_tensor("ones1f", [1, 128], F32))
        oneM = stk.enter_context(nc.sbuf_tensor("oneM", [128, 128], BF16))
        onecol = stk.enter_context(nc.sbuf_tensor("onecol", [128, 1], F32))
        mask = stk.enter_context(nc.sbuf_tensor("mask", [128, 128], F32))
        b1s = stk.enter_context(nc.sbuf_tensor("b1s", [128, FC], F32))
        b2s = stk.enter_context(nc.sbuf_tensor("b2s", [128, DC], F32))
        rows_b = stk.enter_context(nc.sbuf_tensor("rows_b", [128, SUB], BF16))
        rows_c = stk.enter_context(nc.sbuf_tensor("rows_c", [128, SUB], BF16))
        ldec = stk.enter_context(nc.sbuf_tensor("ldec", [1, 1], F32))
        lncol = stk.enter_context(nc.sbuf_tensor("lncol", [128, 1], F32))
        thr = stk.enter_context(nc.sbuf_tensor("thr", [1, 2], F32))
        epsc = stk.enter_context(nc.sbuf_tensor("epsc", [128, 1], F32))
        thr_col = stk.enter_context(nc.sbuf_tensor("thr_col", [128, 2], F32))
        nthr_col = stk.enter_context(nc.sbuf_tensor("nthr_col", [128, 2], F32))

        cdiff = stk.enter_context(nc.sbuf_tensor("cdiff", [128, 128], F32))
        cind = stk.enter_context(nc.sbuf_tensor("cind", [128, 128], F32))

        P = Prog(nc, sems)

        def replay(phase):
            with nc.Block() as block:
                P.replay(phase, block)

        def bc3(ps, nch):
            return bass.AP(ps, 0, [[SUB, 128], [0, nch], [1, SUB]])

        P.set_phase("p1")
        P.op("y", lambda e: e.dma_start(b1s[:], b1_t.ap()), writes=["b1s"], dma=True, dma_sem="yc")
        P.op("y", lambda e: e.dma_start(b2s[:], b2_t.ap()), writes=["b2s"], dma=True, dma_sem="yc")
        P.op("y", lambda e: e.dma_start(ldec[:], ldec_t.ap()), writes=["ldec"], dma=True, dma_sem="yc")
        P.op("y", lambda e: e.dma_start(cdiff[:], cdiff_t.ap()), writes=["cdiff"], dma=True, dma_sem="yc")
        P.op("y", lambda e: e.dma_start(cind[:], cind_t.ap()), writes=["cind"], dma=True, dma_sem="yc")
        for _r in ("b1s", "b2s", "ldec", "cdiff", "cind"):
            P.res[_r][0] = ("yc", P.cnt["yc"])
        P.op("v", lambda e: e.memset(ones1[:], 1.0), writes=["ones1"])
        P.op("v", lambda e: e.memset(ones1f[:], 1.0), writes=["ones1f"])
        P.op("v", lambda e: e.memset(oneM[:], 1.0 / D), writes=["oneM"])
        P.op("v", lambda e: e.memset(onecol[:], 1.0), writes=["onecol"])
        P.op("v", lambda e: e.memset(epsc[:], EPS), writes=["epsc"])
        P.op("s", lambda e: e.activation(thr[0:1, 0:1], ldec[:], AF.Sigmoid, bias=0.0, scale=1.0),
             reads=["ldec"], writes=["thr"])
        P.op("s", lambda e: e.activation(thr[0:1, 1:2], thr[0:1, 0:1], AF.Ln, bias=0.0, scale=1.0),
             reads=["thr"], writes=["thr"])
        P.op("t", lambda e: e.matmul(psm[:, 0:1], ones1f[:], thr[0:1, 1:2], start=True, stop=True),
             reads=["thr", "ones1f"], writes=["psm"])
        P.op("v", lambda e: e.tensor_copy(lncol[:], psm[:, 0:1]), reads=["psm"], writes=["lncol"])
        P.op("v", lambda e: e.tensor_scalar(mask[:], cdiff[:], lncol[:], None, ALU.mult),
             reads=["cdiff", "lncol"], writes=["mask"])
        P.op("s", lambda e: e.activation(cdiff[:], mask[:], AF.Exp, bias=0.0, scale=1.0),
             reads=["mask"], writes=["cdiff"])
        P.op("v", lambda e: e.tensor_tensor(mask[:], cdiff[:], cind[:], ALU.mult),
             reads=["cdiff", "cind"], writes=["mask"])


        with contextlib.ExitStack() as p1:
            ws = {}
            for nm, dt_ in (("wq", wq_t), ("wk", wk_t), ("wv", wv_t), ("wg", wg_t), ("wo", wo_t)):
                ws[nm] = p1.enter_context(nc.sbuf_tensor(nm + "s", [128, DC, D], BF16))
            xf = [p1.enter_context(nc.sbuf_tensor(f"xf{j}", [128, DC, SUB], F32)) for j in (0, 1)]
            xbs = [p1.enter_context(nc.sbuf_tensor(f"xbs{j}", [128, DC, SUB], BF16)) for j in (0, 1)]
            sq = p1.enter_context(nc.sbuf_tensor("sq", [128, DC, SUB], BF16))
            xns = [p1.enter_context(nc.sbuf_tensor(f"xns{j}", [128, DC, SUB], BF16)) for j in (0, 1)]
            qs = p1.enter_context(nc.sbuf_tensor("qs", [128, DC, SUB], BF16))
            ks = p1.enter_context(nc.sbuf_tensor("ks", [128, DC, SUB], BF16))
            gs_ = p1.enter_context(nc.sbuf_tensor("gs_", [128, DC, SUB], BF16))
            vs = p1.enter_context(nc.sbuf_tensor("vs", [128, 4, D], BF16))
            sm = p1.enter_context(nc.sbuf_tensor("sm", [128, 4, 128], BF16))
            stg = [p1.enter_context(nc.sbuf_tensor(f"stg{j}", [128, 1024], F32)) for j in range(3)]
            wt = [p1.enter_context(nc.sbuf_tensor(f"wt{j}", [128, 1024], BF16)) for j in (0, 1)]
            pacc = p1.enter_context(nc.sbuf_tensor("pacc", [128, 2], F32))
            pcol = p1.enter_context(nc.sbuf_tensor("pcol", [128, 1], F32))

            P.set_phase("p1")

            def x_ap_early(i):
                return bass.AP(x_t, i * SUB, [[T, 128], [128 * T, DC], [1, SUB]])

            def load_xb(j):
                P.op("g", lambda e, j=j: e.dma_start(xbs[j % 2][:], x_ap_early(j)),
                     writes=[f"xbs{j % 2}"], dma=True, dma_sem=f"gb{j % 2}")

            load_xb(0)
            if nsub > 1:
                load_xb(1)
            for wi, (nm, dt_) in enumerate((("wq", wq_t), ("wk", wk_t), ("wg", wg_t),
                                            ("wv", wv_t), ("wo", wo_t))):
                ap = bass.AP(dt_, 0, [[D, 128], [128 * D, DC], [1, D]])
                P.op("g", lambda e, s=ws[nm], ap=ap: e.dma_start(s[:], ap),
                     writes=[nm], dma=True, dma_sem=f"gw{wi}")

            psteps = []
            tiles = []
            for dc in range(8):
                for qg in range(4):
                    ap_in = bass.AP(w1_t, dc * 128 * FF + qg * 1024, [[FF, 128], [1, 1024]])
                    ap_out = bass.AP(w1q, qg * D * 1024 + dc * 128 * 1024, [[1024, 128], [1, 1024]])
                    tiles.append((0, ap_in, ap_out))
            for fc in range(32):
                ap_in = bass.AP(w2_t, fc * 128 * D, [[D, 128], [1, 1024]])
                ap_out = bass.AP(w2q, fc * 128 * D, [[D, 128], [1, 1024]])
                tiles.append((1, ap_in, ap_out))
            ntile = len(tiles)

            def p1_load(tix):
                w, ap_in, _ = tiles[tix]
                tb = tix % 3
                P.op("g", lambda e, s=stg[tb], ap=ap_in: e.dma_start(s[:], ap),
                     writes=[f"stg{tb}"], dma=True, dma_sem=f"gs{tb}")

            def p1_red(tix):
                tb = tix % 3
                w = tiles[tix][0]
                P.op("v", lambda e, tb=tb: e.tensor_reduce(
                    pcol[:], stg[tb][:], AX.X, ALU.add, apply_absolute_value=True),
                    reads=[f"stg{tb}"], writes=["pcol"])
                P.op("v", lambda e, w=w: e.tensor_tensor(
                    pacc[:, w:w + 1], pacc[:, w:w + 1], pcol[:], ALU.add),
                    reads=["pcol", "pacc"], writes=["pacc"])

            for tix in range(2):
                psteps.append(lambda tix=tix: p1_load(tix))
            for tix in range(ntile):
                psteps.append(lambda tix=tix: p1_red(tix))
                if tix + 2 < ntile:
                    psteps.append(lambda tix=tix: p1_load(tix + 2))

            def theta_chain():
                P.op("t", lambda e: e.matmul(psm[0:1, 0:2], onecol[:], pacc[:], start=True, stop=True),
                     reads=["pacc", "onecol"], writes=["psm"])
                P.op("v", lambda e: e.tensor_scalar(thr[:], psm[0:1, 0:2], 0.5 / (FF * D), 0.5e-8,
                                                    ALU.mult, ALU.max),
                     reads=["psm"], writes=["thr"])
                P.op("t", lambda e: e.matmul(psm[:, 0:2], ones1f[:], thr[:], start=True, stop=True),
                     reads=["thr", "ones1f"], writes=["psm"])
                P.op("v", lambda e: e.tensor_copy(thr_col[:], psm[:, 0:2]), reads=["psm"], writes=["thr_col"])
                P.op("v", lambda e: e.tensor_scalar(nthr_col[:], thr_col[:], -1.0, None, ALU.mult),
                     reads=["thr_col"], writes=["nthr_col"])
            psteps.append(theta_chain)

            def p2_load(tix):
                tb = tix % 3
                _, ap_in, _ = tiles[tix]
                P.op("g", lambda e, s=stg[tb], ap=ap_in: e.dma_start(s[:], ap),
                     writes=[f"stg{tb}"], dma=True, dma_sem=f"gs{tb}")

            def p2_tern(tix):
                w, _, ap_out = tiles[tix]
                tb = tix % 3
                wb2 = tix % 2
                P.op("v", lambda e, tb=tb, wb2=wb2, w=w: e.tensor_scalar(
                    wt[wb2][:], stg[tb][:], nthr_col[:, w:w + 1], None, ALU.is_lt),
                    reads=[f"stg{tb}", "nthr_col"], writes=[f"wt{wb2}"])
                P.op("v", lambda e, tb=tb, wb2=wb2, w=w: e.scalar_tensor_tensor(
                    wt[wb2][:], stg[tb][:], thr_col[:, w:w + 1], wt[wb2][:], ALU.is_gt, ALU.subtract),
                    reads=[f"stg{tb}", "thr_col", f"wt{wb2}"], writes=[f"wt{wb2}"])
                P.op("g", lambda e, wb2=wb2, ap=ap_out: e.dma_start(ap, wt[wb2][:]),
                     reads=[f"wt{wb2}"], writes=["w1q" if w == 0 else "w2q"], dma=True, dma_sem=f"gt{wb2}")

            for tix in range(2):
                psteps.append(lambda tix=tix: p2_load(tix))
            for tix in range(ntile):
                psteps.append(lambda tix=tix: p2_tern(tix))
                if tix + 2 < ntile:
                    psteps.append(lambda tix=tix: p2_load(tix + 2))

            pq = list(psteps)
            per_sub = -(-len(pq) // max(1, nsub - 1))

            def x_ap(i):
                return bass.AP(x_t, i * SUB, [[T, 128], [128 * T, DC], [1, SUB]])

            def xmid_ap(i):
                return bass.AP(xmid, i * SUB, [[T, 128], [128 * T, DC], [1, SUB]])

            def load_x(i):
                P.op("y", lambda e, i=i: e.dma_start(xf[i % 2][:], x_ap(i)),
                     writes=[f"xf{i % 2}"], dma=True, dma_sem=f"yx{i % 2}")

            load_x(0)
            if nsub > 1:
                load_x(1)

            def mmst(e, src_, dst):
                last = None
                for kc in range(DC):
                    last = e.matmul(dst[:], oneM[:], src_[:, kc, :],
                                    start=(kc == 0), stop=(kc == DC - 1))
                return last

            def bc3sb(t_):
                return bass.AP(t_, 0, [[SUB, 128], [0, DC], [1, SUB]])

            def ln_a(j):
                xbj, xbr = xbs[j % 2], f"xbs{j % 2}"
                P.op("v", lambda e, xbj=xbj: e.tensor_tensor(sq[:], xbj[:], xbj[:], ALU.mult),
                     reads=[xbr], writes=["sq"])
                P.op("t", lambda e, xbj=xbj: mmst(e, xbj, psm), reads=[xbr, "oneM"], writes=["psm"])
                P.op("t", lambda e: mmst(e, sq, stsq), reads=["sq", "oneM"], writes=["stsq"])
                P.op("s", lambda e: e.activation(rows_b[:], psm[:], AF.Square, bias=0.0, scale=1.0),
                     reads=["psm"], writes=["rows_b"])
                P.op("v", lambda e: e.scalar_tensor_tensor(rows_c[:], stsq[:], 0.0, rows_b[:],
                                                           ALU.add, ALU.subtract),
                     reads=["stsq", "rows_b"], writes=["rows_c"])
                P.op("s", lambda e: e.activation(rows_b[:], rows_c[:], AF.Ln, bias=epsc[:], scale=1.0),
                     reads=["rows_c", "epsc"], writes=["rows_b"])
                P.op("s", lambda e: e.activation(rows_c[:], rows_b[:], AF.Exp, bias=0.0, scale=-0.5),
                     reads=["rows_b"], writes=["rows_c"])

            def ln_b(j):
                pass

            def ln_c(j):
                xbj, xbr = xbs[j % 2], f"xbs{j % 2}"
                P.op("v", lambda e, xbj=xbj: e.tensor_tensor(sq[:], xbj[:], bc3(psm, DC), ALU.subtract),
                     reads=[xbr, "psm"], writes=["sq"])

            def ln_d(j):
                xnj, xnr = xns[j % 2], f"xns{j % 2}"
                P.op("v", lambda e, xnj=xnj: e.tensor_tensor(xnj[:], sq[:], bc3sb(rows_c), ALU.mult),
                     reads=["sq", "rows_c"], writes=[xnr])

            def ln_cluster(j):
                ln_a(j); ln_b(j); ln_c(j); ln_d(j)

            def dump_fm(nm, sb, res):
                ap = bass.AP(dbg[nm], 0, [[SUB, 128], [128 * SUB, DC], [1, SUB]])
                P.op("g", lambda e, sb=sb, ap=ap: e.dma_start(ap, sb[:]), reads=[res], dma=True, dma_sem="gdbg")

            def dump_raw(nm, sb_ap, res):
                P.op("g", lambda e, sb_ap=sb_ap, nm=nm: e.dma_start(dbg[nm].ap(), sb_ap), reads=[res], dma=True, dma_sem="gdbg")

            if debug:
                dump_raw("dmask", mask[:], "mask")

            P.op("v", lambda e: e.memset(pacc[:], 0.0), writes=["pacc"])
            ln_cluster(0)
            if nsub > 1:
                ln_cluster(1)
            if nsub > 2:
                load_xb(2)

            for i in range(nsub):
                buf = i % 2
                xfb = xf[buf]
                xfr = f"xf{buf}"
                xn = xns[buf]
                xnr = f"xns{buf}"
                if i + 3 < nsub:
                    load_xb(i + 3)
                if debug and i == 0:
                    dump_fm("dxn", xn, xnr)
                P.op("v", lambda e, xfb=xfb, xn=xn: e.tensor_tensor(xfb[:], xfb[:], xn[:], ALU.add),
                     reads=[xnr, xfr], writes=[xfr])
                if debug and i == 0:
                    dump_fm("dxf1", xfb, xfr)
                    ap2 = bass.AP(dbg["dxf2"], 0, [[SUB, 128], [128 * SUB, DC], [1, SUB]])
                    P.op("y", lambda e, xfb=xfb, ap2=ap2: e.dma_start(ap2, xfb[:]), reads=[xfr], dma=True, dma_sem="ydbg")
                pbanks = [(psp[0], "psp0"), (psp[1], "psp1"), (pso[0], "pso0"),
                          (pso[1], "pso1"), (pso2, "pso2")]
                def proj(wsb, wres, dst, dres, sigmoid=False, per_chunk=False):
                    for do in range(DC):
                        pb, pbr = pbanks[do % 5]
                        def mmg(e, wsb=wsb, do=do, pb=pb, xn=xn):
                            last = None
                            for kc in range(DC):
                                last = e.matmul(pb[:], wsb[:, kc, do * 128:(do + 1) * 128],
                                                xn[:, kc, :], start=(kc == 0), stop=(kc == DC - 1))
                            return last
                        P.op("t", mmg, reads=[xnr, wres], writes=[pbr])
                        wr = [f"{dres}{do}"] if per_chunk else [dres]
                        if sigmoid:
                            P.op("s", lambda e, do=do, pb=pb, dst=dst: e.activation(
                                dst[:, do, :], pb[:], AF.Sigmoid, bias=0.0, scale=1.0),
                                reads=[pbr], writes=wr)
                        else:
                            P.op("s", lambda e, do=do, pb=pb, dst=dst: e.copy(dst[:, do, :], pb[:]),
                                 reads=[pbr], writes=wr)
                proj(ws["wq"], "wq", qs, "qs", per_chunk=True)
                proj(ws["wk"], "wk", ks, "ks")
                if debug and i == 0:
                    dump_fm("dq", qs, "qs")
                    dump_fm("dk", ks, "ks")
                    dump_fm("dg", gs_, "gs")
                for grp in range(4):
                    for half in range(2):
                        pb, pbr = pbanks[(grp * 2 + half) % 5]
                        def mmv(e, grp=grp, half=half, pb=pb, xn=xn):
                            last = None
                            for kc in range(DC):
                                last = e.matmul(pb[:], xn[:, kc, grp * 128:(grp + 1) * 128],
                                                ws["wv"][:, kc, half * 512:(half + 1) * 512],
                                                start=(kc == 0), stop=(kc == DC - 1))
                            return last
                        P.op("t", mmv, reads=[xnr, "wv"], writes=[pbr])
                        P.op("s", lambda e, grp=grp, half=half, pb=pb: e.copy(
                            vs[:, grp, half * 512:(half + 1) * 512], pb[:]),
                            reads=[pbr], writes=["vs"])
                proj(ws["wg"], "wg", gs_, "gs", sigmoid=True)
                if i + 2 < nsub:
                    ln_a(i + 2)
                def mms(e):
                    last = None
                    for grp in range(4):
                        for kc in range(DC):
                            last = e.matmul(pss[:, grp, :], ks[:, kc, grp * 128:(grp + 1) * 128],
                                            qs[:, kc, grp * 128:(grp + 1) * 128],
                                            start=(kc == 0), stop=(kc == DC - 1))
                    return last
                P.op("t", mms, reads=[f"qs{c}" for c in range(DC)] + ["ks"], writes=["pss"])
                P.op("v", lambda e: e.tensor_tensor(
                    sm[:], pss[:], bass.AP(mask, 0, [[128, 128], [0, 4], [1, 128]]), ALU.mult),
                    reads=["pss", "mask"], writes=["sm"])
                if debug and i == 0:
                    dump_raw("dv", vs[:], "vs")
                    dump_raw("dsm", sm[:], "sm")
                obanks = [(pso[0][:], "pso0"), (pso[1][:], "pso1"),
                          (bass.AP(pss, 0, [[SUB, 128], [1, SUB]]), "pss"),
                          (pso2[:], "pso2")]
                for dc in range(DC):
                    pb, pbr = obanks[dc % 4]
                    def mmo(e, dc=dc, pb=pb):
                        last = None
                        for grp in range(4):
                            last = e.matmul(pb[:, grp * 128:(grp + 1) * 128],
                                            vs[:, grp, dc * 128:(dc + 1) * 128],
                                            sm[:, grp, :], start=True, stop=True)
                        return last
                    P.op("t", mmo, reads=["vs", "sm"], writes=[pbr])
                    P.op("v", lambda e, dc=dc, pb=pb: e.tensor_tensor(
                        qs[:, dc, :], pb[:], gs_[:, dc, :], ALU.mult),
                        reads=[pbr, "gs"], writes=[f"qs{dc}"])
                for do in range(DC):
                    pb = psp[do % 2]
                    pbr = f"psp{do % 2}"
                    for kc in range(DC):
                        P.op("t", lambda e, do=do, kc=kc, pb=pb: e.matmul(
                            pb[:], ws["wo"][:, kc, do * 128:(do + 1) * 128],
                            qs[:, kc, :], start=(kc == 0), stop=(kc == DC - 1)),
                            reads=[f"qs{kc}", "wo"], writes=[pbr])
                    P.op("v", lambda e, do=do, pb=pb, xfb=xfb: e.tensor_tensor(
                        xfb[:, do, :], pb[:], xfb[:, do, :], ALU.add),
                        reads=[pbr, xfr], writes=[xfr])
                if i + 2 < nsub:
                    ln_c(i + 2)
                    ln_d(i + 2)
                if debug and i == 0:
                    dump_fm("dgo", qs, "qs0")
                P.op("g", lambda e, i=i, xfb=xfb: e.dma_start(xmid_ap(i), xfb[:]),
                     reads=[xfr], writes=[f"xm{i}"], dma=True, dma_sem=f"gm{i % 2}")
                if i + 2 < nsub:
                    load_x(i + 2)
                for _ in range(per_sub):
                    if pq:
                        pq.pop(0)()
            while pq:
                pq.pop(0)()

            fence = []
            for s in ("gw0", "gw1", "gw2", "gw3", "gw4", "gb0", "gb1",
                      "gs0", "gs1", "gs2", "gt0", "gt1", "gm0", "gm1", "gdbg"):
                P.raw_wait("g", s, P.cnt[s])
                fence.append((s, P.cnt[s]))
            for s in ("yc", "yx0", "yx1", "ydbg"):
                P.raw_wait("y", s, P.cnt[s])
                fence.append((s, P.cnt[s]))
            for s in ("st", "sv", "ss", "sg"):
                fence.append((s, P.cnt[s]))

        with contextlib.ExitStack() as p2:
            w2s = p2.enter_context(nc.sbuf_tensor("w2s", [128, FC, D], BF16))
            w1g = [p2.enter_context(nc.sbuf_tensor(f"w1g{j}", [128, DC, 1024], BF16)) for j in (0, 1)]
            xb2 = [p2.enter_context(nc.sbuf_tensor(f"xb2{j}", [128, DC, SUB], BF16)) for j in (0, 1)]
            sq2 = p2.enter_context(nc.sbuf_tensor("sq2", [128, DC, SUB], BF16))
            xn2 = p2.enter_context(nc.sbuf_tensor("xn2", [128, DC, SUB], BF16))
            hs = p2.enter_context(nc.sbuf_tensor("hs", [128, FC, SUB], BF16))
            outst = p2.enter_context(nc.sbuf_tensor("outst", [128, DC, SUB], F32))

            P.set_phase("p2")

            def load_w1g(gix):
                tb = gix % 2
                ap = bass.AP(w1q, (gix % 4) * D * 1024, [[1024, 128], [128 * 1024, DC], [1, 1024]])
                P.op("g", lambda e, tb=tb, ap=ap: e.dma_start(w1g[tb][:], ap),
                     reads=["w1q"], writes=[f"w1g{tb}"], dma=True, dma_sem=f"gg{tb}", extra_deps=fence)

            def load_xb2(i):
                P.op("y", lambda e, i=i: e.dma_start(xb2[i % 2][:], xmid_ap(i)),
                     reads=[f"xm{i}"], writes=[f"xb2{i % 2}"], dma=True, dma_sem=f"yb{i % 2}", extra_deps=fence)

            def out_ap(i):
                return bass.AP(out_t, i * SUB, [[T, 128], [128 * T, DC], [1, SUB]])

            load_xb2(0)
            load_w1g(0)
            if nsub > 1:
                load_xb2(1)
            load_w1g(1)
            for qrt in range(4):
                ap = bass.AP(w2q, qrt * 8 * 128 * D, [[D, 128], [128 * D, 8], [1, D]])
                P.op("y", lambda e, qrt=qrt, ap=ap: e.dma_start(w2s[:, qrt * 8:(qrt + 1) * 8, :], ap),
                     reads=["w2q"], writes=[f"w2s{qrt}"], dma=True,
                     dma_sem="yw2" + "abcd"[qrt], extra_deps=fence)

            def ln2_square(xbuf, xres):
                P.op("v", lambda e, xbuf=xbuf: e.tensor_tensor(sq2[:], xbuf[:], xbuf[:], ALU.mult),
                     reads=[xres], writes=["sq2"])

            def ln2_stats(xbuf, xres):
                P.op("t", lambda e, xbuf=xbuf: mmst(e, xbuf, psm), reads=[xres, "oneM"], writes=["psm"])
                P.op("t", lambda e: mmst(e, sq2, stsq), reads=["sq2", "oneM"], writes=["stsq"])

            def ln2_stage1(xbuf, xres):
                ln2_square(xbuf, xres)
                ln2_stats(xbuf, xres)

            def ln2_rows_v():
                P.op("s", lambda e: e.activation(rows_b[:], psm[:], AF.Square, bias=0.0, scale=1.0),
                     reads=["psm"], writes=["rows_b"])
                P.op("v", lambda e: e.scalar_tensor_tensor(rows_c[:], stsq[:], 0.0, rows_b[:],
                                                           ALU.add, ALU.subtract),
                     reads=["stsq", "rows_b"], writes=["rows_c"])

            def ln2_rstd_s():
                P.op("s", lambda e: e.activation(rows_b[:], rows_c[:], AF.Ln, bias=epsc[:], scale=1.0),
                     reads=["rows_c", "epsc"], writes=["rows_b"])
                P.op("s", lambda e: e.activation(rows_c[:], rows_b[:], AF.Exp, bias=0.0, scale=-0.5),
                     reads=["rows_b"], writes=["rows_c"])

            def ln2_rows():
                ln2_rows_v()
                ln2_rstd_s()

            def ln2_m_sub(xbuf, xres):
                P.op("v", lambda e, xbuf=xbuf: e.tensor_tensor(sq2[:], xbuf[:], bc3(psm, DC), ALU.subtract),
                     reads=[xres, "psm"], writes=["sq2"])

            def ln2_r_mul():
                P.op("v", lambda e: e.tensor_tensor(xn2[:], sq2[:], bc3sb(rows_c), ALU.mult),
                     reads=["sq2", "rows_c"], writes=["xn2"])

            ln2_stage1(xb2[0], "xb20")
            ln2_rows()
            ln2_m_sub(xb2[0], "xb20")
            ln2_r_mul()

            for i in range(nsub):
                buf = i % 2
                xb2b = xb2[buf]
                xb2r = f"xb2{buf}"
                nxt = i + 1 < nsub
                nbuf = (i + 1) % 2
                if nxt:
                    ln2_square(xb2[nbuf], f"xb2{nbuf}")
                w1banks = [(psp[0][:], "psp0"), (psp[1][:], "psp1"),
                           (bass.AP(pss, 0, [[SUB, 128], [1, SUB]]), "pss"), (pso2[:], "pso2")]
                for fg in range(4):
                    gix = i * 4 + fg
                    wb = w1g[gix % 2]
                    wbr = f"w1g{gix % 2}"
                    for fcol in range(8):
                        fc = fg * 8 + fcol
                        pb, pbr = w1banks[fcol % 4]
                        def mmw1(e, wb=wb, fcol=fcol, pb=pb):
                            last = None
                            for kc in range(DC):
                                last = e.matmul(pb[:], wb[:, kc, fcol * 128:(fcol + 1) * 128],
                                                xn2[:, kc, :], start=(kc == 0), stop=(kc == DC - 1))
                            return last
                        P.op("t", mmw1, reads=["xn2", wbr], writes=[pbr])
                        P.op("s", lambda e, fc=fc, pb=pb: e.activation(
                            hs[:, fc, :], pb[:], AF.Gelu, bias=b1s[:, fc:fc + 1], scale=1.0),
                            reads=[pbr, "b1s"], writes=[f"hs{fc // 8}"])
                    if gix + 2 < nsub * 4:
                        load_w1g(gix + 2)
                    if nxt:
                        if fg == 0:
                            ln2_stats(xb2[nbuf], f"xb2{nbuf}")
                            ln2_rows_v()
                        elif fg == 2:
                            ln2_m_sub(xb2[nbuf], f"xb2{nbuf}")
                        elif fg == 3:
                            ln2_rstd_s()
                            ln2_r_mul()
                for do in range(DC):
                    pb = pso[do % 2]
                    pbr = f"pso{do % 2}"
                    for qrt in range(4):
                        def mmw2(e, do=do, pb=pb, qrt=qrt):
                            last = None
                            for fc in range(qrt * 8, (qrt + 1) * 8):
                                last = e.matmul(pb[:], w2s[:, fc, do * 128:(do + 1) * 128],
                                                hs[:, fc, :], start=(fc == 0), stop=(fc == FC - 1))
                            return last
                        P.op("t", mmw2, reads=[f"hs{qrt}", f"w2s{qrt}"], writes=[pbr])
                    P.op("v", lambda e, do=do, pb=pb, xb2b=xb2b: e.scalar_tensor_tensor(
                        outst[:, do, :], pb[:], b2s[:, do:do + 1], xb2b[:, do, :], ALU.add, ALU.add),
                        reads=[pbr, "b2s", xb2r], writes=[f"outst{do // 4}"])
                    if do == 3:
                        apo = bass.AP(out_t, i * SUB, [[T, 128], [128 * T, 4], [1, SUB]])
                        P.op("y", lambda e, apo=apo: e.dma_start(apo, outst[:, 0:4, :]),
                             reads=["outst0"], writes=[f"outA{i}"], dma=True, dma_sem="yo0")
                if i + 2 < nsub:
                    load_xb2(i + 2)
                apo = bass.AP(out_t, i * SUB + 4 * 128 * T, [[T, 128], [128 * T, 4], [1, SUB]])
                P.op("y", lambda e, apo=apo: e.dma_start(apo, outst[:, 4:8, :]),
                     reads=["outst1"], writes=[f"outB{i}"], dma=True, dma_sem="yo1")

            for s in ("yo0", "yo1", "yb0", "yb1", "yw2a", "yw2b", "yw2c", "yw2d"):
                P.raw_wait("y", s, P.cnt[s])
            with nc.Block() as block:
                reg = {"t": block.tensor, "v": block.vector, "s": block.scalar,
                       "g": block.gpsimd, "y": block.sync}
                for e in ENGS:
                    items = P.items["p1"][e] + P.items["p2"][e]
                    reg[e](lambda eng, items=items: [it(eng) for it in items])
    return nc



def _host_consts():
    idx = np.arange(128)
    same = (idx[:, None] // 32) == (idx[None, :] // 32)
    diff = (idx[None, :] - idx[:, None]).astype(np.float32) * same
    ind = (same & (idx[None, :] >= idx[:, None])).astype(np.float32)
    return np.ascontiguousarray(diff.astype(np.float32)), np.ascontiguousarray(ind)


def kernel(L0_attn_x, L0_attn_W_q, L0_attn_W_k, L0_attn_W_v, L0_attn_W_g,
           L0_attn_W_o, L0_attn_log_decay, L0_ffn_W1, L0_ffn_b1, L0_ffn_W2,
           L0_ffn_b2):
    x = np.asarray(L0_attn_x, dtype=np.float32)
    B, Tb, Dm = x.shape
    per = B // NCORE
    cdiff, cind = _host_consts()
    common = {
        "wq_t": np.ascontiguousarray(np.asarray(L0_attn_W_q, np.float32).T),
        "wk_t": np.ascontiguousarray(np.asarray(L0_attn_W_k, np.float32).T),
        "wv_t": np.ascontiguousarray(np.asarray(L0_attn_W_v, np.float32).T),
        "wg_t": np.ascontiguousarray(np.asarray(L0_attn_W_g, np.float32).T),
        "wo_t": np.ascontiguousarray(np.asarray(L0_attn_W_o, np.float32).T),
        "w1_t": np.ascontiguousarray(np.asarray(L0_ffn_W1, np.float32).T),
        "w2_t": np.ascontiguousarray(np.asarray(L0_ffn_W2, np.float32).T),
        "b1_t": np.ascontiguousarray(np.asarray(L0_ffn_b1, np.float32).reshape(FC, 128).T),
        "b2_t": np.ascontiguousarray(np.asarray(L0_ffn_b2, np.float32).reshape(DC, 128).T),
        "ldec_t": np.ascontiguousarray(np.asarray(L0_attn_log_decay, np.float32)[0:1].reshape(1, 1)),
        "cdiff_t": cdiff,
        "cind_t": cind,
    }
    in_maps = []
    for c in range(NCORE):
        xs = np.ascontiguousarray(x[c * per:(c + 1) * per].reshape(per * Tb, Dm).T)
        m = dict(common)
        m["x_t"] = xs
        in_maps.append(m)
    nsub = (per * Tb) // SUB
    nc = build_core_program(nsub)
    res = run_bass_kernel_spmd(nc, in_maps, core_ids=list(range(NCORE)))
    outs = [np.asarray(res.results[c]["out_t"], dtype=np.float32).T.reshape(per, Tb, Dm)
            for c in range(NCORE)]
    return np.concatenate(outs, axis=0)
